# version 02 (final)
# baseline (speedup 1.0000x reference)
import numpy as np
from contextlib import ExitStack
from types import SimpleNamespace

import concourse.bass as bass
import concourse.mybir as mybir
from concourse.bass_utils import run_bass_kernel_spmd

F32 = mybir.dt.float32
BF16 = mybir.dt.bfloat16
I32 = mybir.dt.int32
I16 = mybir.dt.int16
AF = mybir.ActivationFunctionType
ALU = mybir.AluOpType
AX = mybir.AxisListType

D = 1024
T = 4096
CH = 8
TB = 512
NTB = T // TB
NCORE = 8
EPS = 1e-5
H = 16
DH = 64
FF = 4096
P4_ZSCALE = 0.25
P5_LIMIT = None
P5_SUB = 3
P5_DEBUG = None
P4_DEBUG = None
YAT_BASE = float(np.sqrt(np.float32(D)) / np.log(np.float32(1 + D)))

PHASES = ("p1", "p2", "p3", "p4", "p5", "p6")
GROUPS = [["p1", "p2", "p3", "p4", "p5", "p6"]]

ENGINES = ("sync", "tensor", "vector", "scalar", "gpsimd")


class Em:

    def __init__(self, nc, es):
        self.nc = nc
        self.q = {k: [] for k in ENGINES}
        self.cnt = {}
        self._sems = {}
        self._es = es
        self.trace = []
        self.tq = {k: [] for k in ENGINES}

    def _sem(self, name):
        if name not in self._sems:
            self._sems[name] = self._es.enter_context(self.nc.semaphore(name))
            self.cnt[name] = 0
        return self._sems[name]

    def val(self, name):
        return self.cnt.get(name, 0)

    def op(self, eng, fn, waits=(), inc=None, inc_by=1):
        waits2 = [(s, v) for (s, v) in waits if v > 0]
        waits = [(self._sem(s), v) for (s, v) in waits2]
        sem_obj = self._sem(inc) if inc else None
        after = None
        if inc:
            self.cnt[inc] += inc_by
            after = self.cnt[inc]
        self.q[eng].append((fn, waits, sem_obj, inc_by))
        import traceback
        fr = traceback.extract_stack(limit=4)[0]
        self.tq[eng].append((list(waits2), inc, inc_by if inc else 0,
                             f"{fr.name}:{fr.lineno}"))
        return after

    DMA_SEM = {"sync": "ld", "gpsimd": "gld", "scalar": "sld", "vector": "vld"}

    def dma(self, eng, out, in_, waits=(), inc=None, **kw):
        if inc is None:
            inc = self.DMA_SEM[eng]
        return self.op(
            eng, lambda e: e.dma_start(out=out, in_=in_, **kw), waits, inc, 16
        )

    def load_barrier(self):
        ws = [(k, self.val(k)) for k in ("ld", "gld", "sld", "sto")]
        self.op("sync", lambda e: e.nop(), waits=ws)
        self.flush()

    def dma_tr(self, eng, out, in_, waits=(), inc="ld"):
        return self.op(
            eng, lambda e: e.dma_start_transpose(out, in_), waits, inc, 16
        )

    def flush(self):
        self.trace.append(self.tq)
        self.tq = {k: [] for k in ENGINES}
        qs = self.q
        self.q = {k: [] for k in ENGINES}
        if not any(qs.values()):
            return
        with self.nc.Block() as blk:
            for eng in ENGINES:
                lst = qs[eng]
                if not lst:
                    continue

                def runner(e, lst=lst):
                    for fn, waits, sem_obj, by in lst:
                        for (s, v) in waits:
                            e.wait_ge(s, v)
                        inst = fn(e)
                        if sem_obj is not None:
                            inst.then_inc(sem_obj, by)

                getattr(blk, eng)(runner)


def mm_group(em, psum_ap, lhsT_list, rhs_list, waits=(), inc="pe", cont=False):
    n = len(lhsT_list)
    out = None
    for i, (lh, rh) in enumerate(zip(lhsT_list, rhs_list)):
        w = waits if i == 0 else ()
        ic = inc if i == n - 1 else None
        st = (i == 0) and not cont
        sp = i == n - 1
        out = em.op(
            "tensor",
            lambda e, lh=lh, rh=rh, st=st, sp=sp: e.matmul(
                psum_ap, lh, rh, start=st, stop=sp
            ),
            w,
            ic,
        )
    return out



def declare_io(nc):
    IN = {}

    def di(name, shape, dt=F32):
        IN[name] = nc.dram_tensor(name, list(shape), dt, kind="ExternalInput").ap()

    di("xT", (D, T))
    di("WrT", (D, D)); di("WkT", (D, D)); di("WvT", (D, D))
    di("mu_r8", (CH, 128)); di("mu_k8", (CH, 128))
    di("W1", (D, D)); di("b1_8", (CH, 128))
    di("W23", (D, 32)); di("b23", (32,))
    di("wq", (D, D)); di("wk", (D, D)); di("wv", (D, D)); di("wo", (D, D))
    di("bq8", (CH, 128)); di("bk8", (CH, 128)); di("bv8", (CH, 128)); di("bo8", (CH, 128))
    di("aqkv", (3,))
    di("gamma8", (CH, 128)); di("beta8", (CH, 128))
    di("fc1_q", (D, FF), I32); di("fc1_scales", (CH, FF)); di("fc1b32", (32, 128))
    di("fc2_q", (FF, D), I32); di("fc2_scales", (32, D)); di("fc2b8", (CH, 128))
    di("eye128", (128, 128)); di("onesrow", (1, 128)); di("ind2", (2, 128))
    di("sccol", (128, 1)); di("ones16", (16, 1)); di("halfmask", (128, 2))
    di("onescol", (128, 1))
    IN["yT"] = nc.dram_tensor("yT", [D, T], F32, kind="ExternalOutput").ap()

    def sc(name, shape, dt):
        IN[name] = nc.dram_tensor(name, list(shape), dt, kind="Internal").ap()

    sc("xh0", (D, T), F32)
    sc("xh1", (D, T), F32)
    sc("uln_scr", (D, T), BF16)
    sc("e_scr", (2, 2, T, T), BF16)
    sc("w1scr", (D, FF), BF16)
    sc("w2scr", (FF, D), BF16)
    sc("wn_scr", (3, 1024), F32)
    sc("xn_scr", (1, T), F32)
    sc("at_scr", (D, T), BF16)
    return IN


def alloc_residents(C):
    nc, es, em, IN = C.nc, C.es, C.em, C.IN
    R = SimpleNamespace()
    R.eye = es.enter_context(nc.sbuf_tensor("r_eye", [128, 128], BF16))
    R.onesrow = es.enter_context(nc.sbuf_tensor("r_ones", [1, 128], BF16))
    R.ind2 = es.enter_context(nc.sbuf_tensor("r_ind2", [2, 128], BF16))
    R.sccol = es.enter_context(nc.sbuf_tensor("r_sccol", [128, 1], BF16))
    R.ones16 = es.enter_context(nc.sbuf_tensor("r_ones16", [16, 1], BF16))
    R.halfmask = es.enter_context(nc.sbuf_tensor("r_halfmask", [128, 2], BF16))
    R.onescol = es.enter_context(nc.sbuf_tensor("r_onescol", [128, 1], BF16))
    for t, k in ((R.eye, "eye128"), (R.onesrow, "onesrow"), (R.ind2, "ind2"),
                 (R.sccol, "sccol"), (R.ones16, "ones16"), (R.halfmask, "halfmask"),
                 (R.onescol, "onescol")):
        em.dma("gpsimd", t[:], IN[k])
    C.R = R
    em.load_barrier()



def make_ln_ws(C, pes, tag):
    nc = C.nc
    e_ = pes.enter_context
    W = SimpleNamespace()
    W.xbf = [e_(nc.sbuf_tensor(f"{tag}_xbf{c}", [128, TB], BF16)) for c in range(CH)]
    W.xsq = [e_(nc.sbuf_tensor(f"{tag}_xsq{c}", [128, TB], BF16)) for c in range(CH)]
    W.rows = e_(nc.sbuf_tensor(f"{tag}_rows", [1, 1024], F32))
    W.rows2 = e_(nc.sbuf_tensor(f"{tag}_rows2", [1, 1024], F32))
    W.rowbf = e_(nc.sbuf_tensor(f"{tag}_rowbf", [1, 1024], BF16))
    W.ps_st = e_(nc.psum_tensor(f"{tag}_psst", [1, 1024], F32))
    W.ps_b = e_(nc.psum_tensor(f"{tag}_psb", [128, 1024], F32))
    W.st_war = []
    W.st_psum_war = []
    W.b_psum_war = []
    return W


def emit_ln_block(C, W, get_x_ap, x_ready, precast=False):
    em, R = C.em, C.R
    for c in range(CH):
        xa = get_x_ap(c)
        if not precast:
            em.op("vector", lambda e, c=c, xa=xa: e.tensor_copy(W.xbf[c][:], xa),
                  waits=list(x_ready) + W.st_war, inc="vec")
        em.op("scalar", lambda e, c=c, xa=xa: e.activation(W.xsq[c][:], xa, AF.Square),
              waits=list(x_ready) + W.st_war, inc="act")
    cast_done = em.val("vec")
    sq_done = em.val("act")
    stat_w = (list(x_ready) if precast else [("vec", cast_done)])
    mm_group(em, W.ps_st[0:1, 0:512],
             [R.sccol[:]] * CH, [W.xbf[c][:] for c in range(CH)],
             waits=stat_w + W.st_psum_war, inc="pe")
    st_pe = mm_group(em, W.ps_st[0:1, 512:1024],
                     [R.sccol[:]] * CH, [W.xsq[c][:] for c in range(CH)],
                     waits=[("act", sq_done)], inc="pe")
    W.st_war = [("pe", st_pe)]
    em.op("vector", lambda e: e.tensor_copy(W.rows[:], W.ps_st[0:1, :]),
          waits=[("pe", st_pe)], inc="vec")
    W.st_psum_war = [("vec", em.val("vec"))]
    em.op("vector", lambda e: e.tensor_tensor(
        W.rows2[0:1, 0:512], W.rows[0:1, 0:512], W.rows[0:1, 0:512], ALU.mult), inc="vec")
    em.op("vector", lambda e: e.tensor_tensor(
        W.rows2[0:1, 512:1024], W.rows[0:1, 512:1024], W.rows2[0:1, 0:512], ALU.subtract), inc="vec")
    em.op("vector", lambda e: e.tensor_scalar(
        W.rows2[0:1, 512:1024], W.rows2[0:1, 512:1024], EPS, None, ALU.add), inc="vec")
    ve_done = em.val("vec")
    em.op("scalar", lambda e: e.activation(
        W.rows2[0:1, 0:512], W.rows2[0:1, 512:1024], AF.Ln),
        waits=[("vec", ve_done)], inc="act")
    em.op("scalar", lambda e: e.activation(
        W.rowbf[0:1, 512:1024], W.rows2[0:1, 0:512], AF.Exp, scale=-0.5), inc="act")
    em.op("scalar", lambda e: e.activation(
        W.rowbf[0:1, 0:512], W.rows[0:1, 0:512], AF.Identity), inc="act")
    rows_done = em.val("act")
    em.op("tensor", lambda e: e.matmul(
        W.ps_b[:, 0:512], C.R.onesrow[:], W.rowbf[0:1, 0:512], start=True, stop=True),
        waits=[("act", rows_done)] + W.b_psum_war, inc="pe")
    bc_pe = em.op("tensor", lambda e: e.matmul(
        W.ps_b[:, 512:1024], C.R.onesrow[:], W.rowbf[0:1, 512:1024], start=True, stop=True),
        inc="pe")
    return W.ps_b[:, 0:512], W.ps_b[:, 512:1024], bc_pe


class XStream:

    def __init__(self, C, pes, tag, xsrc, xdst, nbuf=2):
        self.C = C
        self.xsrc = xsrc
        self.xdst = xdst
        nc = C.nc
        self.nbuf = nbuf
        self.buf = [[pes.enter_context(nc.sbuf_tensor(f"{tag}_x{p}c{c}", [128, TB], F32))
                     for c in range(CH)] for p in range(nbuf)]
        if nbuf == 1:
            self.buf = self.buf + self.buf
        self.ld_val = [0, 0]
        self.war = [[], []]
        self.loaded = -1

    def prefetch(self, tb):
        if tb >= NTB or tb <= self.loaded:
            return
        C, em = self.C, self.C.em
        p = tb % self.nbuf
        t0 = tb * TB
        for c in range(CH):
            em.dma("sync", self.buf[tb % 2][c][:],
                   self.xsrc[128 * c:128 * (c + 1), t0:t0 + TB],
                   waits=self.war[p])
        self.ld_val[p] = em.val("ld")
        self.war[p] = []
        self.loaded = tb

    def ap(self, tb, c):
        return self.buf[tb % 2][c][:]

    def ready(self, tb):
        return [("ld", self.ld_val[tb % self.nbuf])]

    def writeback(self, tb, writer_waits):
        C, em = self.C, self.C.em
        p = tb % self.nbuf
        t0 = tb * TB
        for c in range(CH):
            em.dma("gpsimd", self.xdst[128 * c:128 * (c + 1), t0:t0 + TB],
                   self.buf[tb % 2][c][:], waits=writer_waits, inc="sto")
        self.war[p] = [("sto", em.val("sto"))]



def emit_l0ffn(C, tag, xsrc, xdst):
    em, nc, IN, R = C.em, C.nc, C.IN, C.R
    with ExitStack() as pes:
        e_ = pes.enter_context
        wr = [e_(nc.sbuf_tensor(f"{tag}_wr{k}", [128, D], BF16)) for k in range(CH)]
        wk = [e_(nc.sbuf_tensor(f"{tag}_wk{k}", [128, D], BF16)) for k in range(CH)]
        wv = [e_(nc.sbuf_tensor(f"{tag}_wv{k}", [128, D], BF16)) for k in range(CH)]
        muR = e_(nc.sbuf_tensor(f"{tag}_muR", [128, CH], F32))
        muK = e_(nc.sbuf_tensor(f"{tag}_muK", [128, CH], F32))
        for k in range(CH):
            em.dma("gpsimd", wr[k][:], IN["WrT"][128 * k:128 * (k + 1), :])
            em.dma("gpsimd", wk[k][:], IN["WkT"][128 * k:128 * (k + 1), :])
            em.dma("gpsimd", wv[k][:], IN["WvT"][128 * k:128 * (k + 1), :])
        for c in range(CH):
            em.dma("sync", muR[:, c:c + 1], IN["mu_r8"][c, :].unsqueeze(1))
            em.dma("sync", muK[:, c:c + 1], IN["mu_k8"][c, :].unsqueeze(1))
        em.load_barrier()
        w_ld = 0

        LW = make_ln_ws(C, pes, tag)
        xs = XStream(C, pes, tag, xsrc, xdst)
        uln = [e_(nc.sbuf_tensor(f"{tag}_uln{c}", [128, TB + 1], BF16)) for c in range(CH)]
        dbuf = e_(nc.sbuf_tensor(f"{tag}_d", [128, TB], BF16))
        xr = [e_(nc.sbuf_tensor(f"{tag}_xr{c}", [128, TB], BF16)) for c in range(CH)]
        xk = [e_(nc.sbuf_tensor(f"{tag}_xk{c}", [128, TB], BF16)) for c in range(CH)]
        ka = [e_(nc.sbuf_tensor(f"{tag}_ka{c}", [128, TB], BF16)) for c in range(CH)]
        rsb = [e_(nc.sbuf_tensor(f"{tag}_r{c}", [128, TB], BF16)) for c in range(CH)]
        tmpk = e_(nc.sbuf_tensor(f"{tag}_tmpk", [128, TB], BF16))
        tmix = e_(nc.sbuf_tensor(f"{tag}_tmix", [128, TB], F32))
        psmm = [e_(nc.psum_tensor(f"{tag}_ps{i}", [128, TB], F32)) for i in range(4)]
        ps_free = [[] for _ in range(4)]
        xrxk_war = []
        ka_war = []
        rsb_war = []

        xs.prefetch(0)
        for tb in range(NTB):
            t0 = tb * TB
            xs.prefetch(tb + 1)
            mB, rstdB, bc_pe = emit_ln_block(
                C, LW, lambda c, tb=tb: xs.ap(tb, c), x_ready=xs.ready(tb))
            for c in range(CH):
                if tb == 0:
                    em.op("vector", lambda e, c=c: e.memset(uln[c][:, 0:1], 0.0), inc="vec")
                else:
                    em.op("vector", lambda e, c=c: e.tensor_copy(
                        uln[c][:, 0:1], uln[c][:, TB:TB + 1]), inc="vec")
                em.op("vector", lambda e, c=c, tb=tb: e.tensor_tensor(
                    tmix[:], xs.ap(tb, c), mB, ALU.subtract),
                    waits=[("pe", bc_pe)] + xrxk_war, inc="vec")
                em.op("vector", lambda e, c=c: e.tensor_tensor(
                    uln[c][:, 1:TB + 1], tmix[:], rstdB, ALU.mult), inc="vec")
            LW.b_psum_war = [("vec", em.val("vec"))]
            for c in range(CH):
                em.op("vector", lambda e, c=c: e.tensor_tensor(
                    dbuf[:], uln[c][:, 1:TB + 1], uln[c][:, 0:TB], ALU.subtract),
                    inc="vec")
                em.op("vector", lambda e, c=c: e.scalar_tensor_tensor(
                    xr[c][:], dbuf[:], muR[:, c:c + 1], uln[c][:, 0:TB],
                    ALU.mult, ALU.add), waits=[("ld", w_ld)], inc="vec")
                em.op("vector", lambda e, c=c: e.scalar_tensor_tensor(
                    xk[c][:], dbuf[:], muK[:, c:c + 1], uln[c][:, 0:TB],
                    ALU.mult, ALU.add), inc="vec")
            xrxk_ready = em.val("vec")
            xrxk_war = []
            for oc in range(CH):
                pi = oc % 2
                pv = mm_group(em, psmm[pi][:],
                              [wr[kc][:, 128 * oc:128 * (oc + 1)] for kc in range(CH)],
                              [xr[kc][:] for kc in range(CH)],
                              waits=[("vec", xrxk_ready)] + ps_free[pi] + rsb_war,
                              inc="pe")
                em.op("scalar", lambda e, oc=oc, pi=pi: e.activation(
                    rsb[oc][:], psmm[pi][:], AF.Sigmoid),
                    waits=[("pe", pv)], inc="act")
                ps_free[pi] = [("act", em.val("act"))]
                xrxk_war.append(("pe", pv))
            rsb_war = []
            for oc in range(CH):
                pi = 2 + oc % 2
                pv = mm_group(em, psmm[pi][:],
                              [wk[kc][:, 128 * oc:128 * (oc + 1)] for kc in range(CH)],
                              [xk[kc][:] for kc in range(CH)],
                              waits=[("vec", xrxk_ready)] + ps_free[pi] + ka_war,
                              inc="pe")
                em.op("scalar", lambda e, pi=pi: e.activation(
                    tmpk[:], psmm[pi][:], AF.Relu), waits=[("pe", pv)], inc="act")
                krel = em.val("act")
                em.op("vector", lambda e, oc=oc: e.tensor_tensor(
                    ka[oc][:], tmpk[:], tmpk[:], ALU.mult),
                    waits=[("act", krel)], inc="vec")
                ps_free[pi] = [("vec", em.val("vec"))]
                xrxk_war.append(("pe", pv))
            ka_ready = em.val("vec")
            ka_war = []
            for oc in range(CH):
                pi = oc % 2
                pv = mm_group(em, psmm[pi][:],
                              [wv[kc][:, 128 * oc:128 * (oc + 1)] for kc in range(CH)],
                              [ka[kc][:] for kc in range(CH)],
                              waits=[("vec", ka_ready)] + ps_free[pi], inc="pe")
                em.op("vector", lambda e, oc=oc, pi=pi: e.tensor_tensor(
                    tmix[:], rsb[oc][:], psmm[pi][:], ALU.mult),
                    waits=[("pe", pv)], inc="vec")
                ps_free[pi] = [("vec", em.val("vec"))]
                em.op("vector", lambda e, oc=oc, tb=tb: e.scalar_tensor_tensor(
                    xs.ap(tb, oc), tmix[:], 0.5, xs.ap(tb, oc),
                    ALU.mult, ALU.add), inc="vec")
                ka_war.append(("pe", pv))
            rsb_war = [("vec", em.val("vec"))]
            xs.writeback(tb, [("vec", em.val("vec"))])
        em.flush()


def emit_l0attn(C, tag, xsrc, xdst):
    em, nc, IN, R = C.em, C.nc, C.IN, C.R
    with ExitStack() as pes:
        e_ = pes.enter_context
        w1 = [e_(nc.sbuf_tensor(f"{tag}_w1{k}", [128, D], BF16)) for k in range(CH)]
        w23 = [e_(nc.sbuf_tensor(f"{tag}_w23{k}", [128, 32], BF16)) for k in range(CH)]
        b1t = e_(nc.sbuf_tensor(f"{tag}_b1t", [128, CH], F32))
        b23t = e_(nc.sbuf_tensor(f"{tag}_b23t", [16, 2], F32))
        for k in range(CH):
            em.dma("gpsimd", w1[k][:], IN["W1"][128 * k:128 * (k + 1), :])
            em.dma("gpsimd", w23[k][:], IN["W23"][128 * k:128 * (k + 1), :])
        for c in range(CH):
            em.dma("sync", b1t[:, c:c + 1], IN["b1_8"][c, :].unsqueeze(1))
        em.dma("sync", b23t[:, 0:1], IN["b23"][0:16].unsqueeze(1))
        em.dma("sync", b23t[:, 1:2], IN["b23"][16:32].unsqueeze(1))
        em.load_barrier()
        w_ld = 0

        LW = make_ln_ws(C, pes, tag)
        xs = XStream(C, pes, tag, xsrc, xdst)
        uln = [e_(nc.sbuf_tensor(f"{tag}_uln{c}", [128, TB], BF16)) for c in range(CH)]
        delta = [e_(nc.sbuf_tensor(f"{tag}_dl{c}", [128, TB], BF16)) for c in range(CH)]
        ta = [e_(nc.sbuf_tensor(f"{tag}_ta{p}", [128, TB], BF16)) for p in range(2)]
        tb_ = [e_(nc.sbuf_tensor(f"{tag}_tb{p}", [128, TB], BF16)) for p in range(2)]
        td = [e_(nc.sbuf_tensor(f"{tag}_td{p}", [128, TB], BF16)) for p in range(2)]
        bcb = e_(nc.sbuf_tensor(f"{tag}_bcb", [16, 2 * TB], BF16))
        prod = e_(nc.sbuf_tensor(f"{tag}_prod", [16, TB], BF16))
        srow = e_(nc.sbuf_tensor(f"{tag}_srow", [1, TB], BF16))
        tmix = e_(nc.sbuf_tensor(f"{tag}_tmix", [128, TB], F32))
        tmix2 = e_(nc.sbuf_tensor(f"{tag}_tmix2", [128, TB], F32))
        psmm = [e_(nc.psum_tensor(f"{tag}_ps{i}", [128, TB], F32)) for i in range(2)]
        psbc = e_(nc.psum_tensor(f"{tag}_psbc", [128, 2 * TB], F32))
        ps23B = psbc[0:16, 0:TB]
        ps23C = psbc[0:16, TB:2 * TB]
        ps_s = psbc[64:65, 0:TB]
        ps_sB = psbc[:, TB:2 * TB]
        ps_free = [[] for _ in range(2)]
        uln_war = []
        delta_war = []
        bc_war = []
        tadd_done = [[], []]

        xs.prefetch(0)
        for tb in range(NTB):
            xs.prefetch(tb + 1)
            mB, rstdB, bc_pe = emit_ln_block(
                C, LW, lambda c, tb=tb: xs.ap(tb, c), x_ready=xs.ready(tb))
            for c in range(CH):
                em.op("vector", lambda e, c=c, tb=tb: e.tensor_tensor(
                    tmix[:], xs.ap(tb, c), mB, ALU.subtract),
                    waits=[("pe", bc_pe)] + uln_war, inc="vec")
                em.op("vector", lambda e, c=c: e.tensor_tensor(
                    uln[c][:], tmix[:], rstdB, ALU.mult), inc="vec")
            uln_ready = em.val("vec")
            uln_war = []
            LW.b_psum_war = [("vec", uln_ready)]
            mm_group(em, ps23B,
                     [w23[kc][:, 0:16] for kc in range(CH)],
                     [uln[kc][:] for kc in range(CH)],
                     waits=[("vec", uln_ready), ("ld", w_ld)] + bc_war,
                     inc="pe")
            pv23 = mm_group(em, ps23C,
                            [w23[kc][:, 16:32] for kc in range(CH)],
                            [uln[kc][:] for kc in range(CH)],
                            inc="pe")
            em.op("vector", lambda e: e.tensor_scalar(
                bcb[:, 0:TB], ps23B, b23t[:, 0:1], None, ALU.add),
                waits=[("pe", pv23)], inc="vec")
            em.op("vector", lambda e: e.tensor_scalar(
                bcb[:, TB:2 * TB], ps23C, b23t[:, 1:2], None, ALU.add), inc="vec")
            em.op("vector", lambda e: e.tensor_tensor(
                prod[:], bcb[0:16, 0:TB], bcb[0:16, TB:2 * TB], ALU.mult), inc="vec")
            prod_done = em.val("vec")
            pv_s = em.op("tensor", lambda e: e.matmul(
                ps_s, R.ones16[:], prod[:], start=True, stop=True),
                waits=[("vec", prod_done)], inc="pe")
            em.op("vector", lambda e: e.tensor_copy(srow[:], ps_s),
                  waits=[("pe", pv_s)], inc="vec")
            srow_done = em.val("vec")
            pv_sB = em.op("tensor", lambda e: e.matmul(
                ps_sB[:], R.onesrow[:], srow[:], start=True, stop=True),
                waits=[("vec", srow_done)], inc="pe")
            bc_war = [("vec", srow_done)]
            comb_end = None
            for oc in range(CH):
                pi = oc % 2
                pv = mm_group(em, psmm[pi][:],
                              [w1[kc][:, 128 * oc:128 * (oc + 1)] for kc in range(CH)],
                              [uln[kc][:] for kc in range(CH)],
                              waits=[("vec", uln_ready)] + ps_free[pi] + delta_war,
                              inc="pe")
                em.op("scalar", lambda e, oc=oc, pi=pi: e.activation(
                    ta[pi][:], psmm[pi][:], AF.Abs, bias=b1t[:, oc:oc + 1]),
                    waits=[("pe", pv)] + tadd_done[pi], inc="act")
                em.op("scalar", lambda e, pi=pi: e.activation(
                    tb_[pi][:], ta[pi][:], AF.Exp, scale=-1.0), inc="act")
                em.op("scalar", lambda e, pi=pi: e.activation(
                    ta[pi][:], tb_[pi][:], AF.Ln, bias=1.0), inc="act")
                em.op("scalar", lambda e, oc=oc, pi=pi: e.activation(
                    td[pi][:], psmm[pi][:], AF.Relu, bias=b1t[:, oc:oc + 1]), inc="act")
                sp_done = em.val("act")
                ps_free[pi] = [("act", sp_done)]
                em.op("vector", lambda e, oc=oc, pi=pi: e.tensor_tensor(
                    delta[oc][:], ta[pi][:], td[pi][:], ALU.add),
                    waits=[("act", sp_done)], inc="vec")
                tadd_done[pi] = [("vec", em.val("vec"))]
                em.op("vector", lambda e, oc=oc: e.tensor_tensor(
                    tmix[:], delta[oc][:], ps_sB[:], ALU.mult),
                    waits=[("pe", pv_sB)], inc="vec")
                em.op("vector", lambda e, oc=oc: e.tensor_tensor(
                    tmix2[:], uln[oc][:], tmix[:], ALU.mult), inc="vec")
                em.op("vector", lambda e, oc=oc, tb=tb: e.tensor_tensor(
                    xs.ap(tb, oc), xs.ap(tb, oc), tmix2[:], ALU.add), inc="vec")
                comb_end = em.val("vec")
            delta_war = []
            uln_war = [("vec", comb_end)]
            bc_war.append(("vec", comb_end))
            xs.writeback(tb, [("vec", comb_end)])
        em.flush()


def emit_l1ffn(C, tag, xsrc, xdst):
    em, nc, IN, R = C.em, C.nc, C.IN, C.R
    with ExitStack() as pes:
        e_ = pes.enter_context
        w1sb = [e_(nc.sbuf_tensor(f"{tag}_w1_{k}", [128, FF], BF16)) for k in range(CH)]
        w2sb = [e_(nc.sbuf_tensor(f"{tag}_w2_{k}", [128, D], BF16)) for k in range(32)]
        gam = e_(nc.sbuf_tensor(f"{tag}_gam", [128, CH], F32))
        bet = e_(nc.sbuf_tensor(f"{tag}_bet", [128, CH], F32))
        b1f = e_(nc.sbuf_tensor(f"{tag}_b1f", [128, 32], F32))
        b2fq = e_(nc.sbuf_tensor(f"{tag}_b2fq", [128, CH], F32))
        for c in range(CH):
            em.dma("sync", gam[:, c:c + 1], IN["gamma8"][c, :].unsqueeze(1))
            em.dma("sync", bet[:, c:c + 1], IN["beta8"][c, :].unsqueeze(1))
            em.dma("sync", b2fq[:, c:c + 1], IN["fc2b8"][c, :].unsqueeze(1))
        for j in range(32):
            em.dma("sync", b1f[:, j:j + 1], IN["fc1b32"][j, :].unsqueeze(1))
        pl = em.val("ld")
        em.op("vector", lambda e: e.tensor_scalar(
            b2fq[:], b2fq[:], 0.25, None, ALU.mult), waits=[("ld", pl)], inc="vec")
        b2fq_done = em.val("vec")

        do_dequant = not getattr(C, "have_spill", False)
        if do_dequant:
            C.have_spill = True
            with ExitStack() as qes:
                q_ = qes.enter_context
                qsb = [q_(nc.sbuf_tensor(f"{tag}_qsb{p}", [128, FF], I32)) for p in range(2)]
                tmpf = [q_(nc.sbuf_tensor(f"{tag}_tmpf{p}", [128, TB], F32)) for p in range(2)]
                srow = [q_(nc.sbuf_tensor(f"{tag}_srow{p}", [1, FF], BF16)) for p in range(2)]
                psd = [q_(nc.psum_tensor(f"{tag}_psd{p}", [128, TB], F32)) for p in range(2)]
                q_war = [[], []]
                ps_war = [[], []]

                def dequant(dst_sb, q_src, s_src, n, idx, spill_dst):
                    p = idx % 2
                    em.dma("sync", qsb[p][:, 0:n], q_src, waits=q_war[p])
                    qv = em.val("ld")
                    em.dma("gpsimd", srow[p][0:1, 0:n], s_src)
                    sv = em.val("gld")
                    last = None
                    for j in range(n // TB):
                        sl = slice(TB * j, TB * (j + 1))
                        bc = em.op("tensor", lambda e, sl=sl, p=p: e.matmul(
                            psd[p][:, 0:TB], R.onesrow[:], srow[p][0:1, sl],
                            start=True, stop=True),
                            waits=[("gld", sv)] + ps_war[p], inc="pe")
                        em.op("vector", lambda e, sl=sl, p=p: e.tensor_copy(
                            tmpf[p][:], qsb[p][:, sl]), waits=[("ld", qv)], inc="vec")
                        em.op("vector", lambda e, sl=sl, p=p, dst_sb=dst_sb: e.scalar_tensor_tensor(
                            dst_sb[:, sl], tmpf[p][:], -128.0, psd[p][:, 0:TB],
                            ALU.add, ALU.mult), waits=[("pe", bc)], inc="vec")
                        last = em.val("vec")
                        ps_war[p] = [("vec", last)]
                    q_war[p] = [("vec", last)]
                    em.dma("gpsimd", spill_dst, dst_sb[:, 0:n], waits=[("vec", last)],
                           inc="sto")
                    C.spill_done = em.val("sto")

                for kc in range(CH):
                    dequant(w1sb[kc], IN["fc1_q"][128 * kc:128 * (kc + 1), :],
                            IN["fc1_scales"][kc:kc + 1, :], FF, kc,
                            IN["w1scr"][128 * kc:128 * (kc + 1), :])
                for kc in range(32):
                    dequant(w2sb[kc], IN["fc2_q"][128 * kc:128 * (kc + 1), :],
                            IN["fc2_scales"][kc:kc + 1, :], D, CH + kc,
                            IN["w2scr"][128 * kc:128 * (kc + 1), :])
                em.load_barrier()
        else:
            w1w = [("sto", getattr(C, "spill_done", 0))]
            for kc in range(CH):
                em.dma("sync", w1sb[kc][:], IN["w1scr"][128 * kc:128 * (kc + 1), :],
                       waits=(w1w if kc == 0 else ()))
            for kc in range(32):
                em.dma("sync", w2sb[kc][:], IN["w2scr"][128 * kc:128 * (kc + 1), :])
            em.load_barrier()
        w_ld = 0

        LW = make_ln_ws(C, pes, tag)
        xs = XStream(C, pes, tag, xsrc, xdst, nbuf=1)
        h = [e_(nc.sbuf_tensor(f"{tag}_h{c}", [128, TB], BF16)) for c in range(CH)]
        y = [e_(nc.sbuf_tensor(f"{tag}_y{j}", [128, TB], BF16)) for j in range(24)] \
            + LW.xsq
        tmix = e_(nc.sbuf_tensor(f"{tag}_tmix", [128, TB], F32))
        psmm = [e_(nc.psum_tensor(f"{tag}_ps{i}", [128, TB], F32)) for i in range(4)]
        ps_free = [[] for _ in range(4)]
        u_war = []
        h_war = []
        y_war = []
        x_war = []

        for tb in range(NTB):
            xs.prefetch(tb)
            mB, rstdB, bc_pe = emit_ln_block(
                C, LW, lambda c, tb=tb: xs.ap(tb, c), x_ready=xs.ready(tb))
            st_pe1 = LW.st_war[0][1]
            for c in range(CH):
                em.op("vector", lambda e, c=c, tb=tb: e.tensor_tensor(
                    tmix[:], xs.ap(tb, c), mB, ALU.subtract),
                    waits=[("pe", bc_pe)] + u_war, inc="vec")
                em.op("vector", lambda e, c=c: e.tensor_tensor(
                    LW.xbf[c][:], tmix[:], rstdB, ALU.mult),
                    waits=[("pe", st_pe1)], inc="vec")
            u_ready = em.val("vec")
            u_war = []
            LW.b_psum_war = [("vec", u_ready)]
            mB2, rstdB2, bc_pe2 = emit_ln_block(
                C, LW, lambda c: LW.xbf[c][:], x_ready=[("vec", u_ready)],
                precast=True)
            for c in range(CH):
                em.op("vector", lambda e, c=c: e.tensor_tensor(
                    tmix[:], LW.xbf[c][:], mB2, ALU.subtract),
                    waits=[("pe", bc_pe2)] + h_war, inc="vec")
                em.op("vector", lambda e, c=c: e.tensor_tensor(
                    tmix[:], tmix[:], rstdB2, ALU.mult), inc="vec")
                em.op("vector", lambda e, c=c: e.tensor_scalar(
                    h[c][:], tmix[:], gam[:, c:c + 1], bet[:, c:c + 1],
                    ALU.mult, ALU.add), inc="vec")
            h_ready = em.val("vec")
            h_war = []
            LW.b_psum_war = [("vec", em.val("vec"))]
            ln2_pe = list(LW.st_war)
            for j in range(32):
                pi = j % 2
                pv = mm_group(em, psmm[pi][:],
                              [w1sb[kc][:, 128 * j:128 * (j + 1)] for kc in range(CH)],
                              [h[kc][:] for kc in range(CH)],
                              waits=[("vec", h_ready), ("ld", w_ld)] + ps_free[pi] + y_war,
                              inc="pe")
                silu_w = [("pe", pv)] + (ln2_pe if j >= 24 else [])
                em.op("scalar", lambda e, j=j, pi=pi: e.activation(
                    y[j][:], psmm[pi][:], AF.Silu, bias=b1f[:, j:j + 1]),
                    waits=silu_w, inc="act")
                ps_free[pi] = [("act", em.val("act"))]
                h_war.append(("pe", pv))
            y_ready = em.val("act")
            y_war = []
            comb_end = None
            for oc in range(CH):
                pi = 2 + oc % 2
                pv = mm_group(em, psmm[pi][:],
                              [w2sb[kc][:, 128 * oc:128 * (oc + 1)] for kc in range(32)],
                              [y[kc][:] for kc in range(32)],
                              waits=[("act", y_ready)] + ps_free[pi], inc="pe")
                em.op("vector", lambda e, oc=oc, pi=pi: e.tensor_scalar(
                    tmix[:], psmm[pi][:], P4_ZSCALE, b2fq[:, oc:oc + 1],
                    ALU.mult, ALU.add), waits=[("pe", pv)], inc="vec")
                ps_free[pi] = [("vec", em.val("vec"))]
                if P4_DEBUG == "y":
                    em.op("vector", lambda e, oc=oc, tb=tb: e.tensor_copy(
                        xs.ap(tb, oc), y[oc][:]), waits=[("act", y_ready)], inc="vec")
                elif P4_DEBUG == "y24":
                    em.op("vector", lambda e, oc=oc, tb=tb: e.tensor_copy(
                        xs.ap(tb, oc), y[24 + oc][:]), waits=[("act", y_ready)], inc="vec")
                elif P4_DEBUG == "h":
                    em.op("vector", lambda e, oc=oc, tb=tb: e.tensor_copy(
                        xs.ap(tb, oc), h[oc][:]), inc="vec")
                elif P4_DEBUG == "u":
                    em.op("vector", lambda e, oc=oc, tb=tb: e.tensor_copy(
                        xs.ap(tb, oc), LW.xbf[oc][:]), inc="vec")
                elif P4_DEBUG == "z":
                    em.op("vector", lambda e, oc=oc, tb=tb: e.tensor_copy(
                        xs.ap(tb, oc), tmix[:]), inc="vec")
                else:
                    em.op("vector", lambda e, oc=oc: e.scalar_tensor_tensor(
                        tmix[:], LW.xbf[oc][:], 0.5, tmix[:], ALU.mult, ALU.add),
                        inc="vec")
                    em.op("vector", lambda e, oc=oc, tb=tb: e.tensor_tensor(
                        xs.ap(tb, oc), xs.ap(tb, oc), tmix[:], ALU.add), inc="vec")
                comb_end = em.val("vec")
                y_war.append(("pe", pv))
            u_war = [("vec", comb_end)]
            LW.st_war = LW.st_war + [y_war[-1]]
            xs.writeback(tb, [("vec", comb_end)])
        em.flush()


def emit_l1attn(C, tag, xsrc, xdst):
    em, nc, IN, R = C.em, C.nc, C.IN, C.R
    LNB = float(np.log(YAT_BASE))
    with ExitStack() as pes:
        e_ = pes.enter_context
        bq3 = e_(nc.sbuf_tensor(f"{tag}_bq3", [128, 3 * CH], F32))
        bscl = e_(nc.sbuf_tensor(f"{tag}_bscl", [128, 3 * CH], F32))
        sclB = e_(nc.sbuf_tensor(f"{tag}_sclB", [128, 3], F32))
        wnc = e_(nc.sbuf_tensor(f"{tag}_wnc", [128, 3 * CH], F32))

        with ExitStack() as aes:
            a_ = aes.enter_context
            wqs = [a_(nc.sbuf_tensor(f"{tag}_wq{k}", [128, D], BF16)) for k in range(CH)]
            wks = [a_(nc.sbuf_tensor(f"{tag}_wk{k}", [128, D], BF16)) for k in range(CH)]
            wvs = [a_(nc.sbuf_tensor(f"{tag}_wv{k}", [128, D], BF16)) for k in range(CH)]
            a3 = a_(nc.sbuf_tensor(f"{tag}_a3", [1, 3], F32))
            scl3 = a_(nc.sbuf_tensor(f"{tag}_scl3", [1, 3], BF16))
            for k in range(CH):
                em.dma("gpsimd", wqs[k][:], IN["wq"][128 * k:128 * (k + 1), :])
                em.dma("gpsimd", wks[k][:], IN["wk"][128 * k:128 * (k + 1), :])
                em.dma("gpsimd", wvs[k][:], IN["wv"][128 * k:128 * (k + 1), :])
            for c in range(CH):
                em.dma("sync", bq3[:, c:c + 1], IN["bq8"][c, :].unsqueeze(1))
                em.dma("sync", bq3[:, CH + c:CH + c + 1], IN["bk8"][c, :].unsqueeze(1))
                em.dma("sync", bq3[:, 2 * CH + c:2 * CH + c + 1], IN["bv8"][c, :].unsqueeze(1))
            em.dma("sync", a3[0:1, :], IN["aqkv"][:].unsqueeze(0))
            em.load_barrier()
            em.op("scalar", lambda e: e.activation(a3[0:1, :], a3[0:1, :], AF.Exp,
                                                   scale=LNB), inc="act")
            em.op("vector", lambda e: e.tensor_copy(scl3[0:1, :], a3[0:1, :]),
                  waits=[("act", em.val("act"))], inc="vec")
            sc3v = em.val("vec")

            with ExitStack() as lns:
                l_ = lns.enter_context
                ps_sc3 = l_(nc.psum_tensor(f"{tag}_pssc3", [128, 3], F32))
                pv0 = em.op("tensor", lambda e: e.matmul(
                    ps_sc3[:], R.onesrow[:], scl3[0:1, :], start=True, stop=True),
                    waits=[("vec", sc3v)], inc="pe")
                em.op("vector", lambda e: e.tensor_copy(sclB[:], ps_sc3[:]),
                      waits=[("pe", pv0)], inc="vec")
                for p3 in range(3):
                    em.op("vector", lambda e, p3=p3: e.tensor_scalar(
                        bscl[:, CH * p3:CH * (p3 + 1)], bq3[:, CH * p3:CH * (p3 + 1)],
                        sclB[:, p3:p3 + 1], None, ALU.mult), inc="vec")

                wsq = l_(nc.sbuf_tensor(f"{tag}_wsq", [128, D], BF16))
                wnps = l_(nc.psum_tensor(f"{tag}_wnps", [1, 1024], F32))
                wnrow = l_(nc.sbuf_tensor(f"{tag}_wnrow", [1, 1024], F32))
                wn_war = []
                for p3, ws in enumerate((wqs, wks, wvs)):
                    for kc in range(CH):
                        em.op("scalar", lambda e, ws=ws, kc=kc: e.activation(
                            wsq[:], ws[kc][:], AF.Square),
                            waits=[("pe", em.val("pe"))], inc="act")
                        av = em.val("act")
                        for half in range(2):
                            em.op("tensor", lambda e, half=half, kc=kc: e.matmul(
                                wnps[0:1, 512 * half:512 * (half + 1)], R.onescol[:],
                                wsq[:, 512 * half:512 * (half + 1)],
                                start=(kc == 0), stop=(kc == CH - 1)),
                                waits=[("act", av)] + wn_war, inc="pe")
                    wn_pe = em.val("pe")
                    wv_ = em.op("vector", lambda e: e.tensor_scalar(
                        wnrow[:], wnps[0:1, :], EPS, None, ALU.add),
                        waits=[("pe", wn_pe)], inc="vec")
                    wn_war = [("vec", wv_)]
                    em.dma("gpsimd", IN["wn_scr"][p3:p3 + 1, :], wnrow[0:1, :],
                           waits=[("vec", wv_)], inc="sto")
                em.op("sync", lambda e: e.nop(), waits=[("sto", em.val("sto"))])
                for p3 in range(3):
                    for c in range(CH):
                        em.dma("sync", wnc[:, CH * p3 + c:CH * p3 + c + 1],
                               IN["wn_scr"][p3, 128 * c:128 * (c + 1)].unsqueeze(1))

                LW = make_ln_ws(C, lns, tag)
                xs0 = XStream(C, lns, tag + "u", xsrc, None)
                uln = [l_(nc.sbuf_tensor(f"{tag}_uln{c}", [128, TB], BF16))
                       for c in range(CH)]
                tmixu = l_(nc.sbuf_tensor(f"{tag}_tmixu", [128, TB], F32))
                xnps = l_(nc.psum_tensor(f"{tag}_xnps", [1, TB], F32))
                xnrow = l_(nc.sbuf_tensor(f"{tag}_xnrow", [1, TB], F32))
                uln_war = []
                xn_war = []
                xs0.prefetch(0)
                for tb in range(NTB):
                    xs0.prefetch(tb + 1)
                    mB, rstdB, bc_pe = emit_ln_block(
                        C, LW, lambda c, tb=tb: xs0.ap(tb, c), x_ready=xs0.ready(tb))
                    for c in range(CH):
                        em.op("vector", lambda e, c=c, tb=tb: e.tensor_tensor(
                            tmixu[:], xs0.ap(tb, c), mB, ALU.subtract),
                            waits=[("pe", bc_pe)] + uln_war, inc="vec")
                        em.op("vector", lambda e, c=c: e.tensor_tensor(
                            uln[c][:], tmixu[:], rstdB, ALU.mult), inc="vec")
                    uln_ready = em.val("vec")
                    uln_war = []
                    LW.b_psum_war = [("vec", uln_ready)]
                    xs0.war[tb % 2] = [("vec", uln_ready)] + LW.st_war
                    for c in range(CH):
                        em.op("scalar", lambda e, c=c: e.activation(
                            LW.xsq[c][:], uln[c][:], AF.Square),
                            waits=[("vec", uln_ready)] + LW.st_war, inc="act")
                        av = em.val("act")
                        em.op("tensor", lambda e, c=c: e.matmul(
                            xnps[0:1, :], R.onescol[:], LW.xsq[c][:],
                            start=(c == 0), stop=(c == CH - 1)),
                            waits=[("act", av)] + xn_war, inc="pe")
                    xn_pe = em.val("pe")
                    LW.st_war = LW.st_war + [("pe", xn_pe)]
                    em.op("vector", lambda e: e.tensor_copy(xnrow[:], xnps[0:1, :]),
                          waits=[("pe", xn_pe)], inc="vec")
                    xn_war = [("vec", em.val("vec"))]
                    em.dma("gpsimd", IN["xn_scr"][0:1, tb * TB:(tb + 1) * TB],
                           xnrow[0:1, :], waits=xn_war, inc="sto")
                    for c in range(CH):
                        em.dma("gpsimd",
                               IN["uln_scr"][128 * c:128 * (c + 1), tb * TB:(tb + 1) * TB],
                               uln[c][:], waits=[("vec", uln_ready)], inc="sto")
                        uln_war.append(("sto", em.val("sto")))
                em.op("sync", lambda e: e.nop(), waits=[("sto", em.val("sto"))])
                em.flush()

            ps_mm = [a_(nc.psum_tensor(f"{tag}_psmm{i}", [128, TB], F32)) for i in range(2)]
            ps_pv = [a_(nc.psum_tensor(f"{tag}_pspv{i}", [128, TB], F32)) for i in range(2)]
            ps_sm = a_(nc.psum_tensor(f"{tag}_pssm", [128, TB], F32))
            ps_tn = a_(nc.psum_tensor(f"{tag}_pstn", [2, TB], F32))
            ps_tr = a_(nc.psum_tensor(f"{tag}_pstr", [128, 1024], BF16))
            ub = [[a_(nc.sbuf_tensor(f"{tag}_ub{p}{c}", [128, TB], BF16))
                   for c in range(CH)] for p in range(2)]
            qT = a_(nc.sbuf_tensor(f"{tag}_qT", [128, T], BF16))
            kT = a_(nc.sbuf_tensor(f"{tag}_kT", [128, T], BF16))
            vtk = [[a_(nc.sbuf_tensor(f"{tag}_vtk{p}_{i}", [128, 128], BF16))
                    for i in range(32)] for p in range(2)]
            xnB = a_(nc.sbuf_tensor(f"{tag}_xnB", [1, TB], BF16))
            tdist = a_(nc.sbuf_tensor(f"{tag}_tdist", [128, TB], F32))
            ldist = a_(nc.sbuf_tensor(f"{tag}_ldist", [128, TB], F32))
            sqd = a_(nc.sbuf_tensor(f"{tag}_sqd", [128, TB], BF16))
            edist = a_(nc.sbuf_tensor(f"{tag}_edist", [128, TB], BF16))
            q2 = a_(nc.sbuf_tensor(f"{tag}_q2", [128, TB], BF16))
            qn_sb = a_(nc.sbuf_tensor(f"{tag}_qnsb", [2, TB], F32))
            rsqr = a_(nc.sbuf_tensor(f"{tag}_rsqr", [2, TB], BF16))
            HT = 2048
            t2t = a_(nc.sbuf_tensor(f"{tag}_t2", [128, HT], BF16))
            absD = a_(nc.sbuf_tensor(f"{tag}_absD", [128, HT], BF16))
            lhalf = a_(nc.sbuf_tensor(f"{tag}_lhalf", [128, 1024], F32))
            ytile = a_(nc.sbuf_tensor(f"{tag}_y", [128, HT], BF16))
            argP = [a_(nc.sbuf_tensor(f"{tag}_arg{p}", [128, T], BF16)) for p in range(2)]
            EP = [a_(nc.sbuf_tensor(f"{tag}_E{p}", [128, T], BF16)) for p in range(2)]
            mxP = [a_(nc.sbuf_tensor(f"{tag}_mx{p}", [128, 1], F32)) for p in range(2)]
            zcP = [a_(nc.sbuf_tensor(f"{tag}_zc{p}", [128, 2], F32)) for p in range(2)]
            ztP = [a_(nc.sbuf_tensor(f"{tag}_zt{p}", [128, 1], F32)) for p in range(2)]
            rzP = [a_(nc.sbuf_tensor(f"{tag}_rz{p}", [128, 1], F32)) for p in range(2)]
            et = [a_(nc.sbuf_tensor(f"{tag}_et{p}", [128, 1024], BF16)) for p in range(2)]
            aTp = [a_(nc.sbuf_tensor(f"{tag}_aTp{p}", [128, T], BF16)) for p in range(2)]

            st = {
                "e_done": [0, 0],
                "ewr_war": [[], []],
                "et_war": [[], []],
                "vtk_war": [[], []],
                "sc_qv": 0,
                "q2_war": [],
                "aTp_war": [[], []],
            }
            ps_mm_free = [[], []]
            ps_pv_free = [[], []]
            ps_sm_free = []
            ps_tn_free = []
            ps_tr_free = []
            qkv_war = []
            ub_ld = [0, 0]
            ub_war = [[], []]
            xnB_war = []

            pl = {"b_done": {}, "dma_done": {}, "c_vec": {}, "arg_done": {}}

            def emit_qkv(pair):
                nonlocal ps_mm_free, ps_sm_free, ps_tn_free, ps_tr_free, xnB_war
                pp = pair % 2
                for tb in range(NTB):
                    p = tb % 2
                    sl = slice(tb * TB, (tb + 1) * TB)
                    for c in range(CH):
                        em.dma("sync", ub[p][c][:],
                               IN["uln_scr"][128 * c:128 * (c + 1), sl],
                               waits=ub_war[p])
                    ub_ld[p] = em.val("ld")
                    if P5_SUB >= -1:
                        em.dma("gpsimd", xnB[0:1, :], IN["xn_scr"][0:1, sl],
                               waits=xnB_war)
                        xn_ldv = em.val("gld")
                        bxn = em.op("tensor", lambda e: e.matmul(
                            ps_sm[:], R.onesrow[:], xnB[0:1, :], start=True, stop=True),
                            waits=[("gld", xn_ldv)] + ps_sm_free, inc="pe")
                        xnB_war = [("pe", bxn)]
                    last_mm = None
                    for p3 in range(3):
                        ws = (wqs, wks, wvs)[p3]
                        pi = p3 % 2
                        pvv = mm_group(em, ps_mm[pi][:],
                                       [ws[kc][:, 128 * pair:128 * (pair + 1)]
                                        for kc in range(CH)],
                                       [ub[p][kc][:] for kc in range(CH)],
                                       waits=[("ld", ub_ld[p])] + ps_mm_free[pi] + qkv_war,
                                       inc="pe")
                        last_mm = pvv
                        dst_ap = (qT[:, sl] if p3 == 0 else
                                  (kT[:, sl] if p3 == 1 else q2[:]))
                        if P5_SUB < 0:
                            em.op("scalar", lambda e, pi=pi, dst_ap=dst_ap: e.activation(
                                dst_ap, ps_mm[pi][:], AF.Identity),
                                waits=[("pe", pvv)], inc="act")
                            ps_mm_free[pi] = [("act", em.val("act"))]
                            ydone = em.val("vec")
                            continue
                        em.op("vector", lambda e, p3=p3, pi=pi: e.tensor_scalar(
                            tdist[:], ps_mm[pi][:], -2.0,
                            wnc[:, CH * p3 + pair:CH * p3 + pair + 1],
                            ALU.mult, ALU.add),
                            waits=[("pe", pvv), ("pe", bxn)], inc="vec")
                        ts_v = em.val("vec")
                        em.op("vector", lambda e: e.tensor_tensor(
                            tdist[:], ps_sm[:], tdist[:], ALU.add), inc="vec")
                        td = em.val("vec")
                        ps_sm_free = [("vec", td)]
                        em.op("scalar", lambda e, pi=pi: e.activation(
                            sqd[:], ps_mm[pi][:], AF.Square),
                            waits=[("pe", pvv), ("vec", ts_v)], inc="act")
                        sq_a = em.val("act")
                        ps_mm_free[pi] = [("act", sq_a), ("vec", td)]
                        em.op("scalar", lambda e: e.activation(
                            ldist[:], tdist[:], AF.Ln), waits=[("vec", td)], inc="act")
                        em.op("scalar", lambda e: e.activation(
                            edist[:], ldist[:], AF.Exp, scale=-1.0), inc="act")
                        ed_a = em.val("act")
                        em.op("vector", lambda e: e.tensor_tensor(
                            tdist[:], sqd[:], edist[:], ALU.mult),
                            waits=[("act", ed_a)], inc="vec")
                        em.op("vector", lambda e, p3=p3, dst_ap=dst_ap: e.tensor_scalar(
                            dst_ap, tdist[:], sclB[:, p3:p3 + 1],
                            bscl[:, CH * p3 + pair:CH * p3 + pair + 1],
                            ALU.mult, ALU.add),
                            waits=(st["q2_war"] if p3 == 2 else []), inc="vec")
                        ydone = em.val("vec")
                        if p3 < 2 and P5_SUB >= 1:
                            em.op("scalar", lambda e, dst_ap=dst_ap: e.activation(
                                q2[:], dst_ap, AF.Square),
                                waits=[("vec", ydone)] + st["q2_war"], inc="act")
                            q2a = em.val("act")
                            tn = em.op("tensor", lambda e: e.matmul(
                                ps_tn[:], R.halfmask[:], q2[:], start=True, stop=True),
                                waits=[("act", q2a)] + ps_tn_free, inc="pe")
                            st["q2_war"] = [("pe", tn)]
                            if p3 == 0:
                                em.op("vector", lambda e: e.tensor_copy(
                                    qn_sb[:], ps_tn[:]), waits=[("pe", tn)], inc="vec")
                            else:
                                em.op("vector", lambda e: e.tensor_tensor(
                                    qn_sb[:], ps_tn[:], qn_sb[:], ALU.add),
                                    waits=[("pe", tn)], inc="vec")
                            ps_tn_free = [("vec", em.val("vec"))]
                        elif p3 == 2 and P5_SUB >= 2:
                            for i4 in range(4):
                                kt = 4 * tb + i4
                                tr = em.op("tensor", lambda e, i4=i4: e.transpose(
                                    ps_tr[:, 256 * i4:256 * i4 + 128],
                                    q2[:, 128 * i4:128 * (i4 + 1)], R.eye[:]),
                                    waits=[("vec", ydone)] + ps_tr_free, inc="pe")
                                em.op("vector", lambda e, kt=kt, i4=i4, pp=pp: e.tensor_copy(
                                    vtk[pp][kt][:], ps_tr[:, 256 * i4:256 * i4 + 128]),
                                    waits=[("pe", tr)] + st["vtk_war"][pp], inc="vec")
                                ps_tr_free = [("vec", em.val("vec"))]
                                st["q2_war"] = [("pe", tr)]
                    ub_war[p] = [("pe", last_mm)]
                    if P5_SUB < 3:
                        continue
                    em.op("vector", lambda e: e.tensor_scalar(
                        qn_sb[:], qn_sb[:], EPS, None, ALU.add), inc="vec")
                    cv = em.val("vec")
                    em.op("scalar", lambda e: e.activation(
                        qn_sb[:], qn_sb[:], AF.Ln), waits=[("vec", cv)], inc="act")
                    em.op("scalar", lambda e: e.activation(
                        rsqr[:], qn_sb[:], AF.Exp, scale=-0.5), inc="act")
                    rsa = em.val("act")
                    brs = em.op("tensor", lambda e: e.matmul(
                        ps_sm[:], R.ind2[:], rsqr[:], start=True, stop=True),
                        waits=[("act", rsa)] + ps_sm_free, inc="pe")
                    em.op("vector", lambda e, sl=sl: e.tensor_tensor(
                        qT[:, sl], ps_sm[:], qT[:, sl], ALU.mult),
                        waits=[("pe", brs)], inc="vec")
                    ps_sm_free = [("vec", em.val("vec"))]
                st["sc_qv"] = em.val("vec")

            def stageA_half(pair, it, half):
                nonlocal ps_mm_free
                h, qt = divmod(it, 32)
                hb = 64 * h
                par = it % 2
                qsl = slice(128 * qt, 128 * (qt + 1))
                awar = pl["arg_done"].get(it - 2, [])
                if True:
                    for k2 in range(4):
                        pi = k2 % 2
                        kt0 = 2048 * half + 512 * k2
                        pvv = em.op("tensor",
                                    lambda e, kt0=kt0, pi=pi, hb=hb, qsl=qsl: e.matmul(
                                        ps_mm[pi][:], qT[hb:hb + 64, qsl],
                                        kT[hb:hb + 64, kt0:kt0 + 512],
                                        start=True, stop=True),
                                    waits=ps_mm_free[pi] + [("vec", st["sc_qv"])],
                                    inc="pe")
                        em.op("scalar", lambda e, k2=k2, pi=pi: e.activation(
                            t2t[:, 512 * k2:512 * (k2 + 1)], ps_mm[pi][:],
                            AF.Square, scale=1.4142135623730951),
                            waits=[("pe", pvv)], inc="act")
                        ps_mm_free[pi] = [("act", em.val("act"))]
                    t2a = em.val("act")
                    em.op("vector", lambda e: e.tensor_scalar(
                        absD[:], t2t[:], 1.0, None, ALU.subtract),
                        waits=[("act", t2a)], inc="vec")
                    em.op("vector", lambda e: e.tensor_scalar(
                        absD[:].bitcast(I16), absD[:].bitcast(I16),
                        0x7FFF, None, ALU.bitwise_and), inc="vec")
                    em.op("vector", lambda e: e.tensor_scalar(
                        absD[:], absD[:], 0.001953125, None, ALU.max), inc="vec")
                    adv = em.val("vec")
                    for qh in range(2):
                        s2 = slice(1024 * qh, 1024 * (qh + 1))
                        em.op("scalar", lambda e, s2=s2: e.activation(
                            lhalf[:], absD[:, s2], AF.Ln),
                            waits=[("vec", adv)], inc="act")
                        em.op("scalar", lambda e, s2=s2: e.activation(
                            ytile[:, s2], lhalf[:], AF.Exp, scale=-1.0), inc="act")
                    ya = em.val("act")
                    em.op("vector", lambda e: e.tensor_scalar(
                        absD[:], t2t[:], 1.0, None, ALU.is_lt),
                        waits=[("act", ya)], inc="vec")
                    em.op("vector", lambda e, par=par, half=half: e.scalar_tensor_tensor(
                        argP[par][:, 2048 * half:2048 * (half + 1)],
                        absD[:], -0.5, ytile[:], ALU.add, ALU.mult),
                        waits=awar, inc="vec")
            def stageA_reduce(pair, it):
                par = it % 2
                em.op("vector", lambda e, par=par: e.reduce_max(
                    mxP[par][:], argP[par][:], axis=AX.X, negate=True), inc="vec")
                pl.setdefault("a_done", {})[it] = em.val("vec")

            def stageB(pair, it):
                par = it % 2
                ewar = pl["dma_done"].get(it - 2, [])
                zwar = pl["c_vec"].get(it - 2, [])
                for half in range(2):
                    s4 = slice(2048 * half, 2048 * (half + 1))
                    em.op("scalar", lambda e, par=par, s4=s4, half=half: e.activation(
                        EP[par][:, s4], argP[par][:, s4], AF.Exp,
                        bias=mxP[par][:], accum_out=zcP[par][:, half:half + 1]),
                        waits=[("vec", pl["a_done"][it])] + ewar + zwar, inc="act")
                pl.setdefault("b_done", {})[it] = em.val("act")
                pl["arg_done"][it] = [("act", em.val("act"))]

            def stageC1(pair, it):
                par = it % 2
                em.op("vector", lambda e, par=par: e.tensor_tensor(
                    ztP[par][:], zcP[par][:, 0:1], zcP[par][:, 1:2], ALU.add),
                    waits=[("act", pl["b_done"][it])], inc="vec")

            def stageC2(pair, it):
                par = it % 2
                em.op("vector", lambda e, par=par: e.reciprocal(
                    rzP[par][:], ztP[par][:]), inc="vec")

            def stageC3(pair, it):
                h, qt = divmod(it, 32)
                par = it % 2
                pp = pair % 2
                qsl = slice(128 * qt, 128 * (qt + 1))
                em.op("vector", lambda e, par=par: e.tensor_scalar(
                    EP[par][:], EP[par][:], rzP[par][:], None, ALU.mult), inc="vec")
                nv = em.val("vec")
                pl.setdefault("c_vec", {})[it] = [("vec", nv)]
                if P5_DEBUG == "e" and pair == 0 and it == 0:
                    em.op("vector", lambda e, par=par: e.tensor_copy(
                        aTp[0][:], EP[par][:]), inc="vec")
                    em.op("vector", lambda e, par=par: e.tensor_copy(
                        aTp[1][:, 0:1], mxP[par][:]), inc="vec")
                    em.op("vector", lambda e, par=par: e.tensor_copy(
                        aTp[1][:, 1:2], ztP[par][:]), inc="vec")
                    em.op("vector", lambda e, par=par: e.tensor_copy(
                        aTp[1][:, 2:3], rzP[par][:]), inc="vec")
                    em.op("vector", lambda e, par=par: e.tensor_copy(
                        aTp[1][:, 3:5], zcP[par][:, 0:2]), inc="vec")
                    dv = em.val("vec")
                    em.dma("gpsimd", IN["at_scr"][0:128, :], aTp[0][:],
                           waits=[("vec", dv)], inc="sto")
                    em.dma("gpsimd", IN["at_scr"][128:256, :], aTp[1][:],
                           waits=[("vec", dv)], inc="sto")
                em.dma("gpsimd", IN["e_scr"][pp, h, qsl, :], EP[par][:],
                       waits=[("vec", nv)] + st["ewr_war"][pp], inc="sto")
                pl.setdefault("dma_done", {})[it] = [("sto", em.val("sto"))]
                st["e_done"][pp] = em.val("sto")

            def pad_v(n=1):
                for _ in range(n):
                    em.op("vector", lambda e: e.tensor_copy(ytile[:], t2t[:]), inc="vec")

            def emit_softmax(pair):
                for it in range(64):
                    if it >= 2:
                        stageC1(pair, it - 2)
                    stageA_half(pair, it, 0)
                    if it >= 2:
                        stageC2(pair, it - 2)
                    stageA_half(pair, it, 1)
                    stageA_reduce(pair, it)
                    if it >= 2:
                        stageC3(pair, it - 2)
                    if it >= 1:
                        stageB(pair, it - 1)
                stageB(pair, 63)
                for it in (62, 63):
                    stageC1(pair, it)
                    pad_v()
                    stageC2(pair, it)
                    pad_v()
                    stageC3(pair, it)

            def emit_pv(pair):
                nonlocal ps_pv_free
                pp = pair % 2
                par = pair % 2
                for h in range(2):
                    hb = 64 * h
                    for qp in range(4):
                        for kt in range(32):
                            p = kt % 2
                            em.dma_tr("sync", et[p][:],
                                      IN["e_scr"][pp, h, 1024 * qp:1024 * (qp + 1),
                                                  128 * kt:128 * (kt + 1)],
                                      waits=[("sto", st["e_done"][pp])] + st["et_war"][p],
                                      inc="etr")
                            etv = em.val("etr")
                            for j in range(2):
                                em.op("tensor",
                                      lambda e, kt=kt, j=j, p=p, hb=hb, pp=pp: e.matmul(
                                          ps_pv[j][hb:hb + 64, :],
                                          vtk[pp][kt][:, hb:hb + 64],
                                          et[p][:, 512 * j:512 * (j + 1)],
                                          start=(kt == 0), stop=(kt == 31)),
                                      waits=([("etr", etv)] +
                                             (ps_pv_free[j] if kt == 0 else [])),
                                      inc=("pe" if j == 1 else None))
                            st["et_war"][p] = [("pe", em.val("pe"))]
                        pv_pe = em.val("pe")
                        for j in range(2):
                            qb = 2 * qp + j
                            qsl = slice(512 * qb, 512 * (qb + 1))
                            em.op("vector",
                                  lambda e, j=j, hb=hb, qsl=qsl, par=par: e.tensor_copy(
                                      aTp[par][hb:hb + 64, qsl],
                                      ps_pv[j][hb:hb + 64, :]),
                                  waits=[("pe", pv_pe)] + st["aTp_war"][par], inc="vec")
                            ps_pv_free[j] = [("vec", em.val("vec"))]
                st["aTp_war"][par] = []
                av = em.val("vec")
                em.dma("gpsimd", IN["at_scr"][128 * pair:128 * (pair + 1), :],
                       aTp[par][:], waits=[("vec", av)], inc="sto")
                st["aTp_war"][par] = [("sto", em.val("sto"))]
                st["vtk_war"][pp] = [("pe", em.val("pe"))]
                return [("etr", em.val("etr"))]

            prev_c = None
            if P5_LIMIT != "uln":
                for pair in range(CH):
                    emit_qkv(pair)
                    if P5_LIMIT == "qkv":
                        qkv_war[:] = [("pe", em.val("pe")), ("vec", em.val("vec"))]
                        continue
                    if prev_c is not None and P5_LIMIT != "sm":
                        ewar = emit_pv(prev_c)
                        st["ewr_war"][prev_c % 2] = ewar
                    emit_softmax(pair)
                    qkv_war[:] = [("pe", em.val("pe")), ("vec", em.val("vec"))]
                    prev_c = pair
                if P5_LIMIT not in ("qkv", "sm") and prev_c is not None:
                    emit_pv(prev_c)
            em.flush()

        with ExitStack() as oes:
            o_ = oes.enter_context
            wos = [o_(nc.sbuf_tensor(f"{tag}_wo{k}", [128, D], BF16)) for k in range(CH)]
            bot = o_(nc.sbuf_tensor(f"{tag}_bot", [128, CH], F32))
            for k in range(CH):
                em.dma("gpsimd", wos[k][:], IN["wo"][128 * k:128 * (k + 1), :])
            for c in range(CH):
                em.dma("sync", bot[:, c:c + 1], IN["bo8"][c, :].unsqueeze(1))
            em.load_barrier()
            xs1 = XStream(C, oes, tag + "o", xsrc, xdst)
            ab = [[o_(nc.sbuf_tensor(f"{tag}_ab{p}{c}", [128, TB], BF16))
                   for c in range(CH)] for p in range(2)]
            tmo = o_(nc.sbuf_tensor(f"{tag}_tmo", [128, TB], F32))
            pso = [o_(nc.psum_tensor(f"{tag}_pso{i}", [128, TB], F32)) for i in range(2)]
            pso_free = [[], []]
            ab_ld = [0, 0]
            ab_war = [[], []]

            def ab_fetch(tb):
                if tb >= NTB:
                    return
                p = tb % 2
                sl = slice(tb * TB, (tb + 1) * TB)
                for c in range(CH):
                    em.dma("sync", ab[p][c][:],
                           IN["at_scr"][128 * c:128 * (c + 1), sl], waits=ab_war[p])
                ab_ld[p] = em.val("ld")

            xs1.prefetch(0)
            ab_fetch(0)
            for tb in range(NTB):
                xs1.prefetch(tb + 1)
                ab_fetch(tb + 1)
                p = tb % 2
                comb = None
                last_mm = None
                for oc in range(CH):
                    pi = oc % 2
                    pvv = mm_group(em, pso[pi][:],
                                   [wos[kc][:, 128 * oc:128 * (oc + 1)]
                                    for kc in range(CH)],
                                   [ab[p][kc][:] for kc in range(CH)],
                                   waits=pso_free[pi] + [("ld", ab_ld[p])], inc="pe")
                    last_mm = pvv
                    em.op("vector", lambda e, oc=oc, pi=pi: e.tensor_scalar(
                        tmo[:], pso[pi][:], 1.0, bot[:, oc:oc + 1], ALU.mult, ALU.add),
                        waits=[("pe", pvv)] + xs1.ready(tb), inc="vec")
                    pso_free[pi] = [("vec", em.val("vec"))]
                    if P5_DEBUG in ("att", "e"):
                        em.op("vector", lambda e, oc=oc, tb=tb, p=p: e.tensor_copy(
                            xs1.ap(tb, oc), ab[p][oc][:]), inc="vec")
                    else:
                        em.op("vector", lambda e, oc=oc, tb=tb: e.tensor_tensor(
                            xs1.ap(tb, oc), xs1.ap(tb, oc), tmo[:], ALU.add), inc="vec")
                    comb = em.val("vec")
                ab_war[p] = [("pe", last_mm)]
                xs1.writeback(tb, [("vec", comb)])
            em.flush()


EMIT = {"p1": emit_l0ffn, "p2": lambda C, t, s, d: emit_l0attn(C, t, s, d),
        "p3": emit_l0ffn, "p4": lambda C, t, s, d: emit_l1ffn(C, t, s, d),
        "p5": lambda C, t, s, d: emit_l1attn(C, t, s, d),
        "p6": lambda C, t, s, d: emit_l1ffn(C, t, s, d)}


_prog_cache = {}


def build_group_program(gi):
    group = GROUPS[gi]
    nc = bass.Bass("TRN2", target_bir_lowering=False)
    es = ExitStack()
    with es:
        IN = declare_io(nc)
        C = SimpleNamespace(nc=nc, es=es, IN=IN)
        C.em = Em(nc, es)
        alloc_residents(C)
        src = IN["xT"]
        pp = 0
        for i, ph in enumerate(group):
            last = i == len(group) - 1
            dst = IN["yT"] if last else IN[f"xh{pp}"]
            pp ^= 1
            EMIT[ph](C, f"{ph}", src, dst)
            src = dst
        C.em.op("sync", lambda e: e.nop(), waits=[("sto", C.em.val("sto"))])
        C.em.flush()
    return nc


def host_arrays(inputs):
    g = lambda k: np.asarray(inputs[k])
    A = {}
    A["WrT"] = np.ascontiguousarray(g("L0_ffn_W_r").T)
    A["WkT"] = np.ascontiguousarray(g("L0_ffn_W_k").T)
    A["WvT"] = np.ascontiguousarray(g("L0_ffn_W_v").T)
    A["mu_r8"] = np.ascontiguousarray(g("L0_ffn_mu_r").reshape(CH, 128))
    A["mu_k8"] = np.ascontiguousarray(g("L0_ffn_mu_k").reshape(CH, 128))
    A["W1"] = np.ascontiguousarray(g("L0_attn_W1"))
    A["b1_8"] = np.ascontiguousarray(g("L0_attn_b1").reshape(CH, 128))
    A["W23"] = np.ascontiguousarray(np.concatenate(
        [g("L0_attn_W2"), g("L0_attn_W3")], axis=1))
    A["b23"] = np.ascontiguousarray(np.concatenate(
        [g("L0_attn_b2"), g("L0_attn_b3")]))
    for nm, key in (("wq", "L1_attn_wq"), ("wk", "L1_attn_wk"),
                    ("wv", "L1_attn_wv"), ("wo", "L1_attn_wo")):
        A[nm] = np.ascontiguousarray(g(key))
    for nm, key in (("bq8", "L1_attn_bq"), ("bk8", "L1_attn_bk"),
                    ("bv8", "L1_attn_bv"), ("bo8", "L1_attn_bo")):
        A[nm] = np.ascontiguousarray(g(key).reshape(CH, 128))
    A["aqkv"] = np.stack([g("L1_attn_aq")[0], g("L1_attn_ak")[0],
                          g("L1_attn_av")[0]]).astype(np.float32)
    A["gamma8"] = np.ascontiguousarray(g("L1_ffn_ln_gamma").reshape(CH, 128))
    A["beta8"] = np.ascontiguousarray(g("L1_ffn_ln_beta").reshape(CH, 128))
    A["fc1_q"] = np.ascontiguousarray(g("L1_ffn_fc1_q").astype(np.int32))
    A["fc1_scales"] = np.ascontiguousarray(g("L1_ffn_fc1_scales"))
    A["fc1b32"] = np.ascontiguousarray(g("L1_ffn_fc1_bias").reshape(32, 128))
    A["fc2_q"] = np.ascontiguousarray(g("L1_ffn_fc2_q").astype(np.int32))
    A["fc2_scales"] = np.ascontiguousarray(g("L1_ffn_fc2_scales"))
    A["fc2b8"] = np.ascontiguousarray(g("L1_ffn_fc2_bias").reshape(CH, 128))
    A["eye128"] = np.eye(128, dtype=np.float32)
    A["onesrow"] = np.ones((1, 128), np.float32)
    ind2 = np.zeros((2, 128), np.float32)
    ind2[0, :64] = 1.0
    ind2[1, 64:] = 1.0
    A["ind2"] = ind2
    A["sccol"] = np.full((128, 1), 1.0 / D, np.float32)
    A["onescol"] = np.ones((128, 1), np.float32)
    A["ones16"] = np.ones((16, 1), np.float32)
    hm = np.zeros((128, 2), np.float32)
    hm[0:64, 0] = 1.0
    hm[64:128, 1] = 1.0
    A["halfmask"] = hm
    return A


def kernel(**inputs):
    x = np.asarray(inputs["L0_attn_x"])
    A = host_arrays(inputs)
    xTs = [np.ascontiguousarray(x[b].T) for b in range(NCORE)]
    for gi in range(len(GROUPS)):
        if gi not in _prog_cache:
            _prog_cache[gi] = build_group_program(gi)
        nc = _prog_cache[gi]
        in_maps = [{**A, "xT": xTs[b]} for b in range(NCORE)]
        res = run_bass_kernel_spmd(nc, in_maps, core_ids=list(range(NCORE)))
        xTs = [np.asarray(r["yT"]) for r in res.results]
    out = np.stack([xt.T for xt in xTs], 0)
    return np.ascontiguousarray(out.astype(np.float32))
